# version 04
# baseline (speedup 1.0000x reference)
import numpy as np
import concourse.bass as bass
import concourse.mybir as mybir
from concourse.bass_utils import run_bass_kernel_spmd

F32 = mybir.dt.float32
BF16 = mybir.dt.bfloat16
AF = mybir.ActivationFunctionType
ALU = mybir.AluOpType
AX = mybir.AxisListType

NCORE = 8
B, T, C = 2, 4096, 2048
H, DH, DSTD, LOWR = 16, 128, 120, 8
HID, LAT, E = 1024, 512, 16
SCALE, THRESH = 2.5, 0.1
TOK = B * T
NT = TOK // NCORE
S128 = float(1.0 / np.sqrt(128.0))
EPS = 1e-5
NEG = -1e30

SB_BASE = 18432
SB_TOP = 229344


class Sem:
    def __init__(self, nc, name):
        self.h = nc.alloc_semaphore(name=name)
        self.n = 0

    def inc(self, inst, k=1):
        inst.then_inc(self.h, k)
        self.n += k
        return inst


def build_program():
    nc = bass.Bass("TRN2", target_bir_lowering=False, debug=False,
                   detect_race_conditions=False)

    def inp(name, shape, dt=F32):
        return nc.dram_tensor(name, list(shape), dt, kind="ExternalInput").ap()

    x_h = inp("x", (TOK, C))
    xown_h = inp("xown", (NT, C))
    xfm_h = inp("xfm", (8, 16, 128, 128))
    wt_attn_h = inp("wt_attn", (16, 128, 768))
    wlow_h = inp("wlow", (32, C))
    wsr_h = inp("wsr", (1, 4))
    wproj_h = inp("wproj", (4, 16, 128, 512))
    sw1_h = inp("sw1", (8, 16, 128, 128))
    swg_h = inp("swg", (8, 16, 128, 128))
    sw2_h = inp("sw2", (8, 128, 2048))
    wdown_h = inp("wdown", (16, 128, 512))
    wup_h = inp("wup", (4, 128, 2048))
    wg_h = inp("wg", (16, 128, 16))
    rw1_h = inp("rw1", (E, 4, 128, 1024))
    rwg_h = inp("rwg", (E, 4, 128, 1024))
    rw2_h = inp("rw2", (E, 8, 128, 512))
    bias_h = inp("biases", (1, 16))
    out_h = nc.dram_tensor("out", [NT, C], F32, kind="ExternalOutput").ap()

    ident_h = nc.inline_tensor(np.eye(128, dtype=np.float32), name="identc")
    mk = np.zeros((128, 2048), dtype=np.float32)
    for v in range(4):
        for p in range(128):
            mk[p, v * 512:(v + 1) * 512] = np.where(np.arange(512) <= v * 128 + p, 0.0, NEG)
    mask_h = nc.inline_tensor(mk, name="maskc")
    m0np = np.zeros((32, 16), dtype=np.float32)
    for f in range(16):
        m0np[f, f] = 1.0
        m0np[16 + f, f] = 1.0
    m0_h = nc.inline_tensor(m0np, name="m0c")

    def sb(name, off, shape, dt):
        nb = int(np.prod(shape[1:])) * mybir.dt.size(dt)
        assert SB_BASE <= off and off + nb <= SB_TOP, (name, off, nb, off + nb)
        return nc.alloc_sbuf_tensor_at(name, list(shape), dt, offset=off)

    mask_s = sb("mask_s", SB_BASE, [128, 2048], F32)
    sm = [SB_BASE + 8192]

    def ssm(name, shape, dt=F32):
        t = sb(name, sm[0], shape, dt)
        sm[0] += ((int(np.prod(shape[1:])) * mybir.dt.size(dt) + 31) // 32) * 32
        assert sm[0] <= SB_BASE + 12288, name
        return t

    identf_s = ssm("identf", [128, 128], F32)
    identb_s = ssm("identb", [128, 128], BF16)
    m0_s = ssm("m0_s", [32, 16], F32)
    wsr_s = ssm("wsr_s", [1, 4], F32)
    rowa_s = ssm("rowa", [1, 128], F32)
    ones1_s = ssm("ones1", [1, 128], F32)
    onesc_s = ssm("onesc", [128, 1], F32)
    abq_s = ssm("abq", [32, 1], F32)
    abk_s = ssm("abk", [32, 1], F32)
    scv_s = ssm("scv", [128, 4], F32)
    bias_s = ssm("bias_s", [1, 16], F32)
    brep_s = ssm("brep", [128, 16], F32)
    csrow_s = ssm("csrow", [1, 16], F32)
    csrep_s = ssm("csrep", [128, 16], F32)
    wden_s = ssm("wden", [128, 128], F32)
    rown_s = ssm("rown", [128, 8], F32)
    nmr_s = ssm("nmr", [128, 8], F32)
    mblk_s = ssm("mblk", [128, 16], F32)
    sblk_s = ssm("sblk", [128, 16], F32)
    negm_s = ssm("negm", [128, 2], F32)
    sump_s = ssm("sump", [128, 2], F32)
    rcp_s = ssm("rcp", [128, 2], F32)
    l_sum = ssm("l_sum", [128, 8], F32)
    l_ssq = ssm("l_ssq", [128, 8], F32)
    l_mean = ssm("l_mean", [128, 8], F32)
    l_var = ssm("l_var", [128, 8], F32)
    l_inv = ssm("l_inv", [128, 8], F32)
    l_r0 = ssm("l_r0", [128, 8], F32)
    l_t1 = ssm("l_t1", [128, 8], F32)
    l_r = ssm("l_r", [128, 8], F32)
    l_nb = ssm("l_nb", [128, 8], F32)
    gt1_s = ssm("gt1", [128, 16], F32)
    gt2_s = ssm("gt2", [128, 16], F32)
    gt3_s = ssm("gt3", [128, 16], F32)
    gsc_s = ssm("gsc", [128, 16], F32)
    gden_s = ssm("gden", [128, 1], F32)

    BIG0 = SB_BASE + 12288
    Z0 = BIG0 + 131072
    QF = sb("QF", BIG0, [128, 2 * TOK], BF16)
    KF = sb("KF", BIG0 + 32768, [128, 2 * TOK], BF16)
    VT = sb("VT", BIG0 + 65536, [128, 2 * TOK], BF16)
    AT0 = BIG0 + 98304
    ATTNO = sb("ATTNO", AT0, [128, 64 * 256], BF16)
    US_A = sb("US_A", AT0, [128, 8192], BF16)
    UE_A = sb("UE_A", AT0, [128, 2 * 4096], BF16)
    ACC_A = sb("ACC_A", AT0 + 16384, [128, 4096], F32)
    WT_A = sb("WT_A", Z0, [128, 16 * 768], BF16)
    XNB = sb("XNB", Z0 + 24576, [128, 16 * 512], BF16)
    XT_A = sb("XT_A", Z0 + 41984, [128, 2 * 2048], F32)
    XNTM = sb("XNTM", Z0 + 58368, [128, 2 * 2048], BF16)
    WLOW = sb("WLOW", Z0 + 41984, [32, 512], F32)
    WLOWS = sb("WLOWS", Z0 + 44032, [32, 512], F32)
    WGS = sb("WGS", Z0, [128, 256], F32)
    XL = sb("XL", Z0 + 1024, [128, 4096], BF16)
    XNOWN = sb("XNOWN", Z0 + 9216, [128, 16384], BF16)
    XFM = sb("XFM", Z0 + 41984, [128, 2 * 2048], F32)
    WDOWN = sb("WDOWN", Z0 + 41984, [128, 8192], BF16)
    SWBUF = sb("SWBUF", Z0 + 41984, [128, 4 * 2048], BF16)
    EWA = sb("EWA", Z0 + 58368, [128, 512], BF16)
    EWB = sb("EWB", Z0 + 59392, [128, 512], BF16)
    RWB = sb("RWB", Z0 + 9216, [128, 2 * 12288], BF16)
    SST = sb("SST", Z0, [128, 2 * 4096], F32)
    PSB = sb("PSB", Z0 + 32768, [128, 2 * 4096], BF16)
    PTS = sb("PTS", Z0 + 49152, [128, 2 * 512], BF16)
    USO = sb("USO", BIG0, [128, 8192], BF16)
    ACCO = sb("ACCO", BIG0 + 16384, [128, 4096], F32)
    ACCLM = sb("ACCLM", BIG0 + 32768, [128, 4096], BF16)
    SW2B = sb("SW2B", BIG0 + 40960, [128, 16384], BF16)
    WUPB = sb("WUPB", BIG0 + 73728, [128, 8192], BF16)
    GS = sb("GS", Z0, [128, 16384], BF16)
    ACM = sb("ACM", BIG0, [128, 16384], BF16)
    ATOK = sb("ATOK", Z0 + 51200, [128, 2 * 2048], BF16)
    WPB = sb("WPB", BIG0 + 90112, [128, 2 * 8192], BF16)
    XSL = sb("XSL", Z0 + 32768, [128, 2 * 512], F32)
    TSL = sb("TSL", Z0 + 36864, [128, 2 * 512], F32)
    OSL = sb("OSL", Z0 + 40960, [128, 2 * 512], F32)

    us_d = nc.dram_tensor("us_d", [128, 8192], BF16, kind="Internal")
    acc_d = nc.dram_tensor("acc_d", [128, 4096], F32, kind="Internal")
    a2ain_d = nc.dram_tensor("a2ain_d", [TOK, 256], BF16, kind="Internal")
    agout_d = nc.dram_tensor("agout_d", [NCORE * TOK, 256], BF16, kind="Internal")

    pb = [nc.alloc_psum_tensor(f"pb{i}", [128, 512], F32) for i in range(8)]

    with nc.Block() as block:
        @block.gpsimd
        def _(_g):
            te, ve, se, sy, gp = nc.tensor, nc.vector, nc.scalar, nc.sync, nc.gpsimd

            s_ldx = Sem(nc, "ldx")
            s_ldc = Sem(nc, "ldc")
            s_ldg = Sem(nc, "ldg")
            s_st = Sem(nc, "st")
            s_cc = Sem(nc, "cc")
            s_pe = Sem(nc, "spe")
            s_pmm = Sem(nc, "pmm")
            s_dve = Sem(nc, "sdve")
            s_act = Sem(nc, "sact")
            s_ptp = Sem(nc, "ptp")
            s_dtp = Sem(nc, "dtp")
            s_dqk = Sem(nc, "dqk")
            s_dv = Sem(nc, "dv")
            s_dxl = Sem(nc, "dxl")
            s_due = Sem(nc, "due")
            s_dacc = Sem(nc, "dacc")
            s_pye = Sem(nc, "pye")
            s_pS = Sem(nc, "pS")
            s_dS = Sem(nc, "dS")
            s_ppt = Sem(nc, "ppt")
            s_aptv = Sem(nc, "aptv")
            s_aexp = Sem(nc, "aexp")
            s_ppv = Sem(nc, "ppv")
            s_do = Sem(nc, "sdo")
            s_dlm = Sem(nc, "dlm")
            s_ag = Sem(nc, "ag")
            s_dcm = Sem(nc, "dcm")
            s_dco = Sem(nc, "dco")

            act_norm_done = {}
            norm_done_dve = {}
            own_norm_dve = {}
            own_norm_done = {}
            proj_mm_after_tb = {}
            gate_mm_done = {}
            gate_z_done = {}
            sw_mm_done = {}
            ye_done = {}
            dS_after_kb = {}
            negm_ready = {}
            exp_ready = {}
            b_pt_done = []
            acm_tp_done = {}
            sxo_mm_done = {}
            out_st_done = {}

            s_ldc.inc(sy.dma_start(out=mask_s[:], in_=mask_h[:]), 16)
            s_ldc.inc(sy.dma_start(out=identf_s[:], in_=ident_h[:]), 16)
            s_ldc.inc(sy.dma_start(out=m0_s[:], in_=m0_h[:]), 16)
            s_ldc.inc(sy.dma_start(out=wsr_s[:], in_=wsr_h[:]), 16)
            s_ldc.inc(sy.dma_start(out=bias_s[:], in_=bias_h[:]), 16)
            ldc_consts = s_ldc.n

            s_ldg.inc(gp.dma_start(out=WT_A[:].rearrange("p (cc f) -> p cc f", cc=16),
                                   in_=wt_attn_h.rearrange("cc p f -> p cc f")), 16)
            ldg_wt = s_ldg.n

            ve.wait_ge(s_ldc.h, ldc_consts)
            se.wait_ge(s_ldc.h, ldc_consts)
            s_dve.inc(ve.tensor_copy(identb_s[:], identf_s[:]))
            ve.drain()
            s_dve.inc(ve.memset(ones1_s[:], 1.0))
            ve.drain()
            s_dve.inc(ve.memset(onesc_s[:], 1.0))
            ve.drain()
            s_dve.inc(ve.tensor_scalar_max(rowa_s[0:1, 0:4], wsr_s[0:1, 0:4], 1e-8))
            ve.drain()
            se.wait_ge(s_dve.h, s_dve.n)
            s_act.inc(se.activation(out=rowa_s[0:1, 0:4], in_=rowa_s[0:1, 0:4], func=AF.Sqrt))
            ve.wait_ge(s_act.h, s_act.n)
            s_dve.inc(ve.tensor_single_scalar(rowa_s[0:1, 4:6], wsr_s[0:1, 2:4], THRESH, ALU.is_gt))
            ve.drain()
            s_dve.inc(ve.tensor_mul(rowa_s[0:1, 6:8], rowa_s[0:1, 4:6], rowa_s[0:1, 2:4]))
            ve.drain()
            s_dve.inc(ve.tensor_mul(rowa_s[0:1, 8:10], rowa_s[0:1, 4:6], rowa_s[0:1, 0:2]))
            ve.drain()
            s_dve.inc(ve.tensor_sub(rowa_s[0:1, 8:10], rowa_s[0:1, 0:2], rowa_s[0:1, 8:10]))
            ve.drain()
            for src, dst in ((8, 32), (9, 40), (6, 48), (7, 56)):
                s_dve.inc(ve.tensor_scalar_mul(rowa_s[0:1, dst:dst + 8],
                                               rowa_s[0:1, src:src + 1].broadcast_to([1, 8]), S128))
                ve.drain()
            for src, dst in ((6, 64), (7, 72), (8, 80), (9, 88)):
                s_dve.inc(ve.tensor_copy(rowa_s[0:1, dst:dst + 8],
                                         rowa_s[0:1, src:src + 1].broadcast_to([1, 8])))
                ve.drain()
            te.wait_ge(s_dve.h, s_dve.n)
            s_pe.inc(te.matmul(pb[6][0:32, 0:1], rowa_s[0:1, 32:64], ones1_s[0:1, 0:1],
                               start=True, stop=True))
            s_pe.inc(te.matmul(pb[7][0:32, 0:1], rowa_s[0:1, 64:96], ones1_s[0:1, 0:1],
                               start=True, stop=True))
            ve.wait_ge(s_pe.h, s_pe.n)
            s_dve.inc(ve.tensor_copy(abq_s[:], pb[6][0:32, 0:1]))
            ve.drain()
            s_dve.inc(ve.tensor_copy(abk_s[:], pb[7][0:32, 0:1]))
            ve.drain()
            s_dve.inc(ve.tensor_copy(csrow_s[0:1, 0:4], rowa_s[0:1, 0:4]))
            ve.drain()
            for col, (wsi, scl) in enumerate(((0, S128), (1, S128), (0, 1.0), (1, 1.0))):
                s_dve.inc(ve.tensor_scalar_mul(rowa_s[0:1, 96:97], csrow_s[0:1, wsi:wsi + 1], scl))
                ve.drain()
                s_dve.inc(ve.tensor_copy(rowa_s[0:1, 0:120],
                                         rowa_s[0:1, 96:97].broadcast_to([1, 120])))
                ve.drain()
                s_dve.inc(ve.memset(rowa_s[0:1, 120:128], 1.0))
                ve.drain()
                te.wait_ge(s_dve.h, s_dve.n)
                s_pe.inc(te.matmul(pb[6][:, 4 + col:5 + col], rowa_s[0:1, 0:128],
                                   ones1_s[0:1, 0:1], start=True, stop=True))
                ve.wait_ge(s_pe.h, s_pe.n)
                s_dve.inc(ve.tensor_copy(scv_s[:, col:col + 1], pb[6][:, 4 + col:5 + col]))
                ve.drain()
            s_pe.inc(te.matmul(pb[7][:, 4:20], ones1_s[0:1, 0:128], bias_s[0:1, 0:16],
                               start=True, stop=True))
            ve.wait_ge(s_pe.h, s_pe.n)
            s_dve.inc(ve.tensor_copy(brep_s[:], pb[7][:, 4:20]))
            ve.drain()
            dve_prep = s_dve.n

            gp.wait_ge(s_ldg.h, ldg_wt)
            for part in range(4):
                if part == 0:
                    sy.wait_ge(s_dve.h, dve_prep)
                else:
                    sy.wait_ge(s_dve.h, wlow_consumed)
                s_ldc.inc(sy.dma_start(out=WLOW[:], in_=wlow_h[:, part * 512:(part + 1) * 512]), 16)
                ve.wait_ge(s_ldc.h, s_ldc.n)
                if part == 0:
                    ve.wait_ge(s_ldg.h, ldg_wt)
                for wi, (ab, outbase) in enumerate(((abq_s, 0), (abk_s, 256))):
                    if part > 0 or wi > 0:
                        ve.wait_ge(s_pe.h, s_pe.n)
                    s_dve.inc(ve.tensor_scalar_mul(WLOWS[:], WLOW[:], ab[:, 0:1]))
                    ve.drain()
                    if wi == 1:
                        wlow_consumed = s_dve.n
                    for q in range(4):
                        cc = part * 4 + q
                        te.wait_ge(s_dve.h, s_dve.n)
                        s_pe.inc(te.matmul(pb[6][:, 36:52], WLOWS[:, q * 128:(q + 1) * 128],
                                           m0_s[:], start=True, stop=True))
                        ve.wait_ge(s_pe.h, s_pe.n)
                        s_dve.inc(ve.tensor_copy(
                            WT_A[:, cc * 768 + outbase + 120:cc * 768 + outbase + 128],
                            pb[6][:, 36:44]))
                        ve.drain()
                        s_dve.inc(ve.tensor_copy(
                            WT_A[:, cc * 768 + outbase + 248:cc * 768 + outbase + 256],
                            pb[6][:, 44:52]))
                        ve.drain()
            dve_wtfix = s_dve.n

            I32 = mybir.dt.int32

            def ln_tile(xt_ap, xn_ap, slot, own_tt=None):
                sl = slice(slot, slot + 1)
                s_dve.inc(ve.reduce_sum(l_sum[:, sl], xt_ap, axis=AX.X))
                ve.drain()
                s_act.inc(se.activation(out=xn_ap, in_=xt_ap, func=AF.Square, accum_out=l_ssq[:, sl]))
                ve.wait_ge(s_act.h, s_act.n)
                s_dve.inc(ve.tensor_scalar_mul(l_mean[:, sl], l_sum[:, sl], 1.0 / 2048.0))
                ve.drain()
                s_dve.inc(ve.tensor_scalar_mul(l_var[:, sl], l_ssq[:, sl], 1.0 / 2048.0))
                ve.drain()
                s_dve.inc(ve.tensor_mul(l_t1[:, sl], l_mean[:, sl], l_mean[:, sl]))
                ve.drain()
                s_dve.inc(ve.tensor_sub(l_var[:, sl], l_var[:, sl], l_t1[:, sl]))
                ve.drain()
                s_dve.inc(ve.tensor_scalar_add(l_var[:, sl], l_var[:, sl], EPS))
                ve.drain()
                rdst = rown_s[:, own_tt:own_tt + 1] if own_tt is not None else l_r[:, sl]
                ndst = nmr_s[:, own_tt:own_tt + 1] if own_tt is not None else l_nb[:, sl]
                vi = l_var[:, sl].bitcast(I32)
                ri = l_inv[:, sl].bitcast(I32)
                s_dve.inc(ve.tensor_single_scalar(ri, vi, 1, ALU.arith_shift_right))
                ve.drain()
                s_dve.inc(ve.tensor_scalar(ri, ri, -1, 0x5f3759df, ALU.mult, ALU.add))
                ve.drain()
                for it in range(2):
                    rr = l_inv[:, sl] if it == 0 else rdst
                    s_dve.inc(ve.tensor_mul(l_t1[:, sl], rr, rr))
                    ve.drain()
                    s_dve.inc(ve.tensor_mul(l_t1[:, sl], l_t1[:, sl], l_var[:, sl]))
                    ve.drain()
                    s_dve.inc(ve.tensor_scalar(l_t1[:, sl], l_t1[:, sl], -0.5, 1.5, ALU.mult, ALU.add))
                    ve.drain()
                    s_dve.inc(ve.tensor_mul(rdst if it == 0 else rdst, rr, l_t1[:, sl]))
                    ve.drain()
                s_dve.inc(ve.tensor_scalar_mul(l_t1[:, sl], l_mean[:, sl], -1.0))
                ve.drain()
                s_dve.inc(ve.tensor_mul(ndst, l_t1[:, sl], rdst))
                ve.drain()
                s_dve.inc(ve.tensor_scalar(xn_ap, xt_ap, rdst, ndst, ALU.mult, ALU.add))

            _scph_lnproj = nc.enter_named_scope("ph_lnproj", False)
            xn_tp_group = 0
            qk_i = 0
            v_i = 0
            pe_first_tp = True
            for tile in range(64):
                tb, xt = tile // 4, tile % 4
                xslot = tile % 2
                xt_ap = XT_A[:, xslot * 2048:(xslot + 1) * 2048]
                if tile < 2:
                    sy.wait_ge(s_dve.h, dve_wtfix)
                else:
                    sy.wait_ge(s_act.h, act_norm_done[tile - 2])
                    sy.wait_ge(s_dve.h, norm_done_dve[tile - 2])
                s_ldx.inc(sy.dma_start(out=xt_ap, in_=x_h[tile * 128:(tile + 1) * 128, :]), 16)
                ve.wait_ge(s_ldx.h, s_ldx.n)
                se.wait_ge(s_ldx.h, s_ldx.n)
                xn_ap = XNTM[:, (tile % 2) * 2048:((tile % 2) + 1) * 2048]
                if tile >= 2:
                    se.wait_ge(s_ptp.h, 4 * (tile - 1))
                    ve.wait_ge(s_ptp.h, 4 * (tile - 1))
                ln_tile(xt_ap, xn_ap, tile % 8)
                act_norm_done[tile] = s_act.n
                norm_done_dve[tile] = s_dve.n
                te.wait_ge(s_dve.h, s_dve.n)
                if pe_first_tp:
                    te.wait_ge(s_dve.h, dve_wtfix)
                    pe_first_tp = False
                if xt == 0 and tb > 0:
                    pass
                for g in range(4):
                    bank = pb[xn_tp_group % 2]
                    if xn_tp_group >= 2:
                        te.wait_ge(s_dtp.h, xn_tp_group - 1)
                    bank_bf = bank[:].bitcast(BF16)
                    for q in range(4):
                        mm = te.transpose(bank_bf[:, q * 128:(q + 1) * 128],
                                          xn_ap[:, (g * 4 + q) * 128:(g * 4 + q + 1) * 128],
                                          identb_s[:])
                    s_ptp.inc(mm)
                    ve.wait_ge(s_ptp.h, s_ptp.n)
                    if xt == 0 and g == 0 and tb > 0:
                        ve.wait_ge(s_pmm.h, proj_mm_after_tb[tb - 1])
                    dst = XNB[:, :].rearrange("p (cc t) -> p cc t", cc=16)[
                        :, g * 4:g * 4 + 4, xt * 128:xt * 128 + 128]
                    s_dtp.inc(ve.tensor_copy(dst, bank_bf[:, 0:512].rearrange("p (a b) -> p a b", a=4)))
                    xn_tp_group += 1
                if xt == 3:
                    t0 = tb * 512
                    te.wait_ge(s_dtp.h, s_dtp.n)
                    for f in range(4):
                        bank = pb[2 + qk_i % 2]
                        if qk_i >= 2:
                            te.wait_ge(s_dqk.h, qk_i - 1)
                        for cc in range(16):
                            mm = te.matmul(bank[:],
                                           WT_A[:, cc * 768 + f * 128:cc * 768 + (f + 1) * 128],
                                           XNB[:, cc * 512:(cc + 1) * 512],
                                           start=(cc == 0), stop=(cc == 15))
                        s_pmm.inc(mm)
                        ve.wait_ge(s_pmm.h, s_pmm.n)
                        dstbuf = QF if f < 2 else KF
                        hl = f % 2
                        s_dqk.inc(ve.tensor_scalar(
                            dstbuf[:, hl * TOK + t0:hl * TOK + t0 + 512], bank[:],
                            scv_s[:, f:f + 1], None, ALU.mult))
                        qk_i += 1
                    for vt in range(4):
                        bank = pb[4 + v_i % 2]
                        if v_i >= 2:
                            te.wait_ge(s_dv.h, v_i - 1)
                        for cc in range(16):
                            mm = te.matmul(bank[:, 0:256],
                                           XNB[:, cc * 512 + vt * 128:cc * 512 + (vt + 1) * 128],
                                           WT_A[:, cc * 768 + 512:cc * 768 + 768],
                                           start=(cc == 0), stop=(cc == 15))
                        s_pmm.inc(mm)
                        ve.wait_ge(s_pmm.h, s_pmm.n)
                        tc = tb * 4 + vt
                        s_dv.inc(ve.tensor_copy(
                            VT[:, :].rearrange("p (hl t) -> p hl t", hl=2)[:, :, tc * 128:(tc + 1) * 128],
                            bank[:, 0:256].rearrange("p (hl d) -> p hl d", hl=2)))
                        v_i += 1
                    proj_mm_after_tb[tb] = s_pmm.n
            proj_dqk_total = s_dqk.n
            proj_dv_total = s_dv.n
            proj_pmm_total = s_pmm.n

            nc.leave_named_scope("ph_lnproj", _scph_lnproj[0], False)
            _scph_own = nc.enter_named_scope("ph_own", False)
            for tt in range(8):
                xslot = tt % 2
                xt_ap = XT_A[:, xslot * 2048:(xslot + 1) * 2048]
                if tt < 2:
                    sy.wait_ge(s_act.h, act_norm_done[62 + tt])
                    sy.wait_ge(s_dve.h, norm_done_dve[62 + tt])
                else:
                    sy.wait_ge(s_act.h, own_norm_done[tt - 2])
                    sy.wait_ge(s_dve.h, own_norm_dve[tt - 2])
                s_ldx.inc(sy.dma_start(out=xt_ap, in_=xown_h[tt * 128:(tt + 1) * 128, :]), 16)
                ve.wait_ge(s_ldx.h, s_ldx.n)
                se.wait_ge(s_ldx.h, s_ldx.n)
                xn_ap = XNTM[:, (tt % 2) * 2048:((tt % 2) + 1) * 2048]
                se.wait_ge(s_ptp.h, 4 * 64 if tt < 2 else 4 * (64 + tt - 1))
                ve.wait_ge(s_ptp.h, 4 * 64 if tt < 2 else 4 * (64 + tt - 1))
                ln_tile(xt_ap, xn_ap, tt % 8, own_tt=tt)
                own_norm_done[tt] = s_act.n
                own_norm_dve[tt] = s_dve.n
                te.wait_ge(s_dve.h, s_dve.n)
                for g in range(4):
                    bank = pb[xn_tp_group % 2]
                    te.wait_ge(s_dtp.h, xn_tp_group - 1)
                    bank_bf = bank[:].bitcast(BF16)
                    for q in range(4):
                        mm = te.transpose(bank_bf[:, q * 128:(q + 1) * 128],
                                          xn_ap[:, (g * 4 + q) * 128:(g * 4 + q + 1) * 128],
                                          identb_s[:])
                    s_ptp.inc(mm)
                    ve.wait_ge(s_ptp.h, s_ptp.n)
                    if tt == 0 and g == 0:
                        ve.wait_ge(s_pmm.h, proj_pmm_total)
                    dst = XNOWN[:, :].rearrange("p (cc t) -> p cc t", cc=16)[
                        :, g * 4:g * 4 + 4, tt * 128:tt * 128 + 128]
                    s_dtp.inc(ve.tensor_copy(dst, bank_bf[:, 0:512].rearrange("p (a b) -> p a b", a=4)))
                    xn_tp_group += 1
            own_done_dtp = s_dtp.n
            own_done_ptp = s_ptp.n

            nc.leave_named_scope("ph_own", _scph_own[0], False)
            _scph_gate = nc.enter_named_scope("ph_gate", False)
            sy.wait_ge(s_pmm.h, proj_pmm_total)
            s_ldc.inc(sy.dma_start(out=WGS[:].rearrange("p (cc e) -> p cc e", cc=16),
                                   in_=wg_h.rearrange("cc p e -> p cc e")), 16)
            ldc_wg = s_ldc.n
            te.wait_ge(s_ldc.h, ldc_wg)
            for cc in range(16):
                mm = te.matmul(pb[6][0:1, 0:16], onesc_s[:], WGS[:, cc * 16:(cc + 1) * 16],
                               start=(cc == 0), stop=(cc == 15))
            s_pe.inc(mm)
            ve.wait_ge(s_pe.h, s_pe.n)
            s_dve.inc(ve.tensor_copy(csrow_s[:], pb[6][0:1, 0:16]))
            te.wait_ge(s_dve.h, s_dve.n)
            s_pe.inc(te.matmul(pb[6][:, 20:36], ones1_s[0:1, 0:128], csrow_s[:], start=True, stop=True))
            ve.wait_ge(s_pe.h, s_pe.n)
            s_dve.inc(ve.tensor_copy(csrep_s[:], pb[6][:, 20:36]))
            gate_csrep = s_dve.n
            for tt in range(8):
                fslot = tt % 2
                fap = XFM[:, fslot * 2048:(fslot + 1) * 2048]
                if tt < 2:
                    sy.wait_ge(s_act.h, own_norm_done[6 + tt])
                    sy.wait_ge(s_dve.h, own_norm_dve[6 + tt])
                else:
                    sy.wait_ge(s_pmm.h, gate_mm_done[tt - 2])
                s_ldc.inc(sy.dma_start(out=fap.rearrange("p (cc t) -> p cc t", cc=16),
                                       in_=xfm_h[tt].rearrange("cc p t -> p cc t")), 16)
                te.wait_ge(s_ldc.h, s_ldc.n)
                if tt >= 1:
                    te.wait_ge(s_dve.h, gate_z_done[tt - 1])
                for cc in range(16):
                    mm = te.matmul(pb[7][:, 0:16], fap[:, cc * 128:(cc + 1) * 128],
                                   WGS[:, cc * 16:(cc + 1) * 16],
                                   start=(cc == 0), stop=(cc == 15))
                s_pmm.inc(mm)
                gate_mm_done[tt] = s_pmm.n
                ve.wait_ge(s_pmm.h, s_pmm.n)
                ve.wait_ge(s_dve.h, gate_csrep)
                s_dve.inc(ve.scalar_tensor_tensor(gt1_s[:], csrep_s[:], nmr_s[:, tt:tt + 1],
                                                  pb[7][:, 0:16], ALU.mult, ALU.add))
                ve.drain()
                gate_z_done[tt] = s_dve.n
                s_dve.inc(ve.tensor_scalar_mul(gt1_s[:], gt1_s[:], rown_s[:, tt:tt + 1]))
                ve.drain()
                se.wait_ge(s_dve.h, s_dve.n)
                s_act.inc(se.activation(out=gt2_s[:], in_=gt1_s[:], func=AF.Sigmoid))
                ve.wait_ge(s_act.h, s_act.n)
                s_dve.inc(ve.tensor_add(gt3_s[:], gt2_s[:], brep_s[:]))
                ve.drain()
                s_dve.inc(ve.memset(gt1_s[:], 0.0))
                ve.drain()
                for k in range(1, 16):
                    w = 16 - k
                    s_dve.inc(ve.tensor_tensor(gsc_s[:, 0:w], gt3_s[:, 0:w], gt3_s[:, k:16], ALU.is_ge))
                    ve.drain()
                    s_dve.inc(ve.tensor_add(gt1_s[:, k:16], gt1_s[:, k:16], gsc_s[:, 0:w]))
                    ve.drain()
                    s_dve.inc(ve.tensor_scalar_add(gt1_s[:, 0:w], gt1_s[:, 0:w], 1.0))
                    ve.drain()
                    s_dve.inc(ve.tensor_sub(gt1_s[:, 0:w], gt1_s[:, 0:w], gsc_s[:, 0:w]))
                    ve.drain()
                s_dve.inc(ve.tensor_single_scalar(gsc_s[:], gt1_s[:], 8.0, ALU.is_lt))
                ve.drain()
                s_dve.inc(ve.tensor_mul(gt3_s[:], gsc_s[:], gt2_s[:]))
                ve.drain()
                s_dve.inc(ve.reduce_sum(gden_s[:], gt3_s[:], axis=AX.X))
                ve.drain()
                s_dve.inc(ve.reciprocal(gden_s[:], gden_s[:]))
                ve.drain()
                s_dve.inc(ve.tensor_scalar(wden_s[:, tt * 16:(tt + 1) * 16], gt3_s[:],
                                           gden_s[:, 0:1], SCALE, ALU.mult, ALU.mult))
                ve.drain()
            gate_done_dve = s_dve.n

            nc.leave_named_scope("ph_gate", _scph_gate[0], False)
            _scph_xl = nc.enter_named_scope("ph_xl", False)
            gp.wait_ge(s_pmm.h, gate_mm_done[7])
            s_ldg.inc(gp.dma_start(out=WDOWN[:].rearrange("p (cc l) -> p cc l", cc=16),
                                   in_=wdown_h.rearrange("cc p l -> p cc l")), 16)
            ldg_wdown = s_ldg.n
            te.wait_ge(s_ldg.h, ldg_wdown)
            te.wait_ge(s_dtp.h, own_done_dtp)
            xl_mm_last = 0
            for i, (lc, th) in enumerate([(a, b) for a in range(4) for b in range(2)]):
                bank = pb[2 + i % 2]
                if i >= 2:
                    te.wait_ge(s_dxl.h, i - 1)
                if i == 0:
                    ve.wait_ge(s_dve.h, dve_wtfix)
                for cc in range(16):
                    mm = te.matmul(bank[:],
                                   WDOWN[:, cc * 512 + lc * 128:cc * 512 + (lc + 1) * 128],
                                   XNOWN[:, cc * 1024 + th * 512:cc * 1024 + (th + 1) * 512],
                                   start=(cc == 0), stop=(cc == 15))
                s_pmm.inc(mm)
                ve.wait_ge(s_pmm.h, s_pmm.n)
                s_dxl.inc(ve.tensor_copy(XL[:, lc * 1024 + th * 512:lc * 1024 + (th + 1) * 512], bank[:]))
            xl_done = s_dxl.n
            xl_pmm_total = s_pmm.n

            nc.leave_named_scope("ph_xl", _scph_xl[0], False)
            _scph_shared = nc.enter_named_scope("ph_shared", False)
            gp.wait_ge(s_pmm.h, xl_pmm_total)
            sw_ld_done = {}

            def sw_load(hc):
                bsl = hc % 2
                s_ldg.inc(gp.dma_start(out=SWBUF[:, bsl * 2048:(bsl + 1) * 2048]
                                       .rearrange("p (cc h) -> p cc h", cc=16), in_=sw1_h[hc].rearrange("cc p h -> p cc h")), 16)
                s_ldg.inc(gp.dma_start(out=SWBUF[:, (2 + bsl) * 2048:(3 + bsl) * 2048]
                                       .rearrange("p (cc h) -> p cc h", cc=16), in_=swg_h[hc].rearrange("cc p h -> p cc h")), 16)
                sw_ld_done[hc] = s_ldg.n

            sw_load(0)
            sw_load(1)
            for hc in range(8):
                bsl = hc % 2
                te.wait_ge(s_ldg.h, sw_ld_done[hc])
                for th in range(2):
                    i = hc * 2 + th
                    bh = pb[0 + i % 2]
                    bg = pb[2 + i % 2]
                    if i >= 2:
                        te.wait_ge(s_due.h, i - 1)
                    if i < 2:
                        te.wait_ge(s_dxl.h, xl_done)
                    for cc in range(16):
                        mm = te.matmul(bh[:], SWBUF[:, bsl * 2048 + cc * 128:bsl * 2048 + (cc + 1) * 128],
                                       XNOWN[:, cc * 1024 + th * 512:cc * 1024 + (th + 1) * 512],
                                       start=(cc == 0), stop=(cc == 15))
                    s_pmm.inc(mm)
                    for cc in range(16):
                        mm = te.matmul(bg[:], SWBUF[:, (2 + bsl) * 2048 + cc * 128:(2 + bsl) * 2048 + (cc + 1) * 128],
                                       XNOWN[:, cc * 1024 + th * 512:cc * 1024 + (th + 1) * 512],
                                       start=(cc == 0), stop=(cc == 15))
                    s_pmm.inc(mm)
                    sw_mm_done[hc] = s_pmm.n
                    ve.wait_ge(s_pmm.h, s_pmm.n)
                    s_dve.inc(ve.tensor_scalar_max(EWA[:], bg[:], 0.0))
                    s_dve.inc(ve.tensor_mul(EWB[:], EWA[:], EWA[:]))
                    s_due.inc(ve.tensor_mul(
                        US_A[:, hc * 1024 + th * 512:hc * 1024 + (th + 1) * 512], bh[:], EWB[:]))
                if hc + 2 < 8:
                    gp.wait_ge(s_pmm.h, sw_mm_done[hc])
                    sw_load(hc + 2)
            sy.wait_ge(s_due.h, s_due.n)
            s_st.inc(sy.dma_start(out=us_d.ap(), in_=US_A[:]), 16)
            us_park = s_st.n
            shared_due_total = s_due.n

            nc.leave_named_scope("ph_shared", _scph_shared[0], False)
            _scph_experts = nc.enter_named_scope("ph_experts", False)
            rw_ld_done = {}

            def rw_load(e):
                bsl = e % 2
                s_ldg.inc(gp.dma_start(out=RWB[:, bsl * 12288:bsl * 12288 + 4096]
                                       .rearrange("p (lc h) -> p lc h", lc=4), in_=rw1_h[e].rearrange("lc p h -> p lc h")), 16)
                s_ldg.inc(gp.dma_start(out=RWB[:, bsl * 12288 + 4096:bsl * 12288 + 8192]
                                       .rearrange("p (lc h) -> p lc h", lc=4), in_=rwg_h[e].rearrange("lc p h -> p lc h")), 16)
                s_ldg.inc(gp.dma_start(out=RWB[:, bsl * 12288 + 8192:bsl * 12288 + 12288]
                                       .rearrange("p (hc l) -> p hc l", hc=8), in_=rw2_h[e].rearrange("hc p l -> p hc l")), 16)
                rw_ld_done[e] = s_ldg.n

            gp.wait_ge(s_due.h, shared_due_total)
            rw_load(0)
            rw_load(1)
            ue_i = 0
            for e in range(E):
                bsl = e % 2
                te.wait_ge(s_ldg.h, rw_ld_done[e])
                for th in range(2):
                    up = th
                    for hc in range(8):
                        i = ue_i
                        ue_i += 1
                        bh = pb[0 + i % 2]
                        bg = pb[2 + i % 2]
                        if i >= 2:
                            te.wait_ge(s_due.h, shared_due_total + i - 1)
                        for lc in range(4):
                            mm = te.matmul(bh[:],
                                           RWB[:, bsl * 12288 + lc * 1024 + hc * 128:bsl * 12288 + lc * 1024 + (hc + 1) * 128],
                                           XL[:, lc * 1024 + th * 512:lc * 1024 + (th + 1) * 512],
                                           start=(lc == 0), stop=(lc == 3))
                        s_pmm.inc(mm)
                        for lc in range(4):
                            mm = te.matmul(bg[:],
                                           RWB[:, bsl * 12288 + 4096 + lc * 1024 + hc * 128:bsl * 12288 + 4096 + lc * 1024 + (hc + 1) * 128],
                                           XL[:, lc * 1024 + th * 512:lc * 1024 + (th + 1) * 512],
                                           start=(lc == 0), stop=(lc == 3))
                        s_pmm.inc(mm)
                        ve.wait_ge(s_pmm.h, s_pmm.n)
                        if e == 0 and th == 0 and hc == 0:
                            ve.wait_ge(s_st.h, us_park)
                        s_dve.inc(ve.tensor_scalar_max(EWA[:], bg[:], 0.0))
                        s_dve.inc(ve.tensor_mul(EWB[:], EWA[:], EWA[:]))
                        s_due.inc(ve.tensor_mul(
                            UE_A[:, up * 4096 + hc * 512:up * 4096 + (hc + 1) * 512], bh[:], EWB[:]))
                    te.wait_ge(s_due.h, s_due.n)
                    for tc in range(4):
                        j = (e * 2 + th) * 4 + tc
                        bank = pb[4 + j % 2]
                        if j >= 2:
                            te.wait_ge(s_dacc.h, j - 1)
                        for hc in range(8):
                            mm = te.matmul(bank[:],
                                           UE_A[:, up * 4096 + hc * 512 + tc * 128:up * 4096 + hc * 512 + (tc + 1) * 128],
                                           RWB[:, bsl * 12288 + 8192 + hc * 512:bsl * 12288 + 8192 + (hc + 1) * 512],
                                           start=(hc == 0), stop=(hc == 7))
                        s_pye.inc(mm)
                        ve.wait_ge(s_pye.h, s_pye.n)
                        tg = th * 4 + tc
                        if e == 0:
                            s_dacc.inc(ve.tensor_scalar(
                                ACC_A[:, tg * 512:(tg + 1) * 512], bank[:],
                                wden_s[:, tg * 16 + e:tg * 16 + e + 1], None, ALU.mult))
                        else:
                            s_dacc.inc(ve.scalar_tensor_tensor(
                                ACC_A[:, tg * 512:(tg + 1) * 512], bank[:],
                                wden_s[:, tg * 16 + e:tg * 16 + e + 1],
                                ACC_A[:, tg * 512:(tg + 1) * 512], ALU.mult, ALU.add))
                ye_done[e] = s_pye.n
                if e + 2 < E:
                    gp.wait_ge(s_pye.h, ye_done[e])
                    rw_load(e + 2)
            sy.wait_ge(s_dacc.h, s_dacc.n)
            s_st.inc(sy.dma_start(out=acc_d.ap(), in_=ACC_A[:]), 16)
            acc_park = s_st.n
            ye_total = s_pye.n

            nc.leave_named_scope("ph_experts", _scph_experts[0], False)
            _scph_attn = nc.enter_named_scope("ph_attn", False)
            te.wait_ge(s_dqk.h, proj_dqk_total)
            te.wait_ge(s_dv.h, proj_dv_total)
            ve.wait_ge(s_pye.h, ye_total)
            ve.wait_ge(s_st.h, acc_park)
            state = {"S": 0, "pt": 0, "ptv": 0, "o": 0, "exp": 0, "bi": 0}

            def emit_A(pair, qt):
                s, hl = pair // 2, pair % 2
                base = hl * TOK + s * 4096
                nkb = qt // 4 + 1
                par = qt % 2
                for kb in range(nkb):
                    bank = (pb[0], pb[1], pb[6])[state["S"] % 3]
                    if state["S"] >= 3:
                        te.wait_ge(s_dS.h, dS_after_kb[state["S"] - 3])
                    mm = te.matmul(bank[:], QF[:, base + qt * 128:base + (qt + 1) * 128],
                                   KF[:, base + kb * 512:base + (kb + 1) * 512],
                                   start=True, stop=True)
                    s_pS.inc(mm)
                    ve.wait_ge(s_pS.h, s_pS.n)
                    sl = SST[:, par * 4096 + kb * 512:par * 4096 + (kb + 1) * 512]
                    if kb == nkb - 1:
                        s_dve.inc(ve.tensor_tensor(sl, bank[:],
                                                   mask_s[:, (qt % 4) * 512:(qt % 4 + 1) * 512], ALU.add))
                        ve.drain()
                        s_dS.inc(ve.tensor_scalar(sl, sl, 1.0, None, ALU.mult, ALU.max,
                                                  accum_out=mblk_s[:, par * 8 + kb:par * 8 + kb + 1]))
                        ve.drain()
                    else:
                        s_dS.inc(ve.tensor_scalar(sl, bank[:], 1.0, None, ALU.mult, ALU.max,
                                                  accum_out=mblk_s[:, par * 8 + kb:par * 8 + kb + 1]))
                        ve.drain()
                    dS_after_kb[state["S"]] = s_dS.n
                    state["S"] += 1
                s_dS.inc(ve.reduce_max(negm_s[:, par:par + 1], mblk_s[:, par * 8:par * 8 + nkb],
                                       axis=AX.X, negate=True))
                ve.drain()
                negm_ready[(pair, qt)] = s_dS.n

            def emit_B(pair, qt):
                s, hl = pair // 2, pair % 2
                base = hl * TOK + s * 4096
                nkb = qt // 4 + 1
                par = qt % 2
                obank = pb[4 + par]
                bi = state["bi"]
                se.wait_ge(s_dS.h, negm_ready[(pair, qt)])
                if bi >= 2:
                    se.wait_ge(s_ppt.h, b_pt_done[bi - 2])
                s_aexp.inc(se.activation(
                    out=PSB[:, par * 4096:par * 4096 + nkb * 512],
                    in_=SST[:, par * 4096:par * 4096 + nkb * 512],
                    func=AF.Exp, bias=negm_s[:, par:par + 1], scale=1.0,
                    accum_out=sump_s[:, par:par + 1]))
                exp_all = s_aexp.n
                ve.wait_ge(s_aexp.h, exp_all)
                s_dve.inc(ve.reciprocal(rcp_s[:, par:par + 1], sump_s[:, par:par + 1]))
                ve.drain()
                for kb in range(nkb):
                    ptb = pb[2 + state["pt"] % 2][:].bitcast(BF16)
                    if kb == 0:
                        te.wait_ge(s_aexp.h, exp_all)
                    if state["pt"] >= 2:
                        te.wait_ge(s_aptv.h, state["pt"] - 1)
                    for sub in range(4):
                        mm = te.transpose(ptb[:, sub * 128:(sub + 1) * 128],
                                          PSB[:, par * 4096 + kb * 512 + sub * 128:par * 4096 + kb * 512 + (sub + 1) * 128],
                                          identb_s[:])
                    s_ppt.inc(mm)
                    state["pt"] += 1
                    pv_slot = state["ptv"] % 2
                    if state["ptv"] % 2 == 0:
                        se.wait_ge(s_ppt.h, s_ppt.n)
                        if state["ptv"] >= 2:
                            se.wait_ge(s_ppv.h, state["ptv"] - 1)
                        s_aptv.inc(se.activation(out=PTS[:, pv_slot * 512:(pv_slot + 1) * 512],
                                                 in_=ptb[:, 0:512], func=AF.Copy))
                    else:
                        ve.wait_ge(s_ppt.h, s_ppt.n)
                        if state["ptv"] >= 2:
                            ve.wait_ge(s_ppv.h, state["ptv"] - 1)
                        s_aptv.inc(ve.tensor_copy(PTS[:, pv_slot * 512:(pv_slot + 1) * 512],
                                                  ptb[:, 0:512]))
                    te.wait_ge(s_aptv.h, s_aptv.n)
                    if kb == 0 and state["o"] >= 2:
                        te.wait_ge(s_do.h, state["o"] - 1)
                    for sub in range(4):
                        mm = te.matmul(obank[:, 0:128],
                                       PTS[:, pv_slot * 512 + sub * 128:pv_slot * 512 + (sub + 1) * 128],
                                       VT[:, base + (kb * 4 + sub) * 128:base + (kb * 4 + sub + 1) * 128],
                                       start=(kb == 0 and sub == 0), stop=(kb == nkb - 1 and sub == 3))
                    s_ppv.inc(mm)
                    state["ptv"] += 1
                b_pt_done.append(s_ppt.n)
                ve.wait_ge(s_ppv.h, s_ppv.n)
                tl = s * 32 + qt
                s_do.inc(ve.tensor_scalar(ATTNO[:, tl * 256 + hl * 128:tl * 256 + hl * 128 + 128],
                                          obank[:, 0:128], rcp_s[:, par:par + 1], None, ALU.mult))
                state["o"] += 1
                state["bi"] += 1

            for pair in range(4):
                emit_A(pair, 0)
                for qt in range(32):
                    if qt < 31:
                        emit_A(pair, qt + 1)
                    emit_B(pair, qt)
                if pair in (1, 3):
                    st0 = (pair // 2) * 32
                    sy.wait_ge(s_do.h, s_do.n)
                    for tl in range(st0, st0 + 32):
                        s_st.inc(sy.dma_start(out=a2ain_d.ap()[tl * 128:(tl + 1) * 128, :],
                                              in_=ATTNO[:, tl * 256:(tl + 1) * 256]), 16)
            attn_done_do = s_do.n
            a2a_in_done = s_st.n

            gp.wait_ge(s_st.h, a2a_in_done)
            s_cc.inc(gp.collective_compute(
                "AllGather", ALU.bypass, replica_groups=[list(range(NCORE))],
                ins=[a2ain_d.ap().opt()], outs=[agout_d.ap().opt()]))

            nc.leave_named_scope("ph_attn", _scph_attn[0], False)
            _scph_moeout = nc.enter_named_scope("ph_moeout", False)
            sy.wait_ge(s_do.h, attn_done_do)
            s_ldc.inc(sy.dma_start(out=USO[:], in_=us_d.ap()), 16)
            s_ldc.inc(sy.dma_start(out=ACCO[:], in_=acc_d.ap()), 16)
            reload_done = s_ldc.n
            gp.wait_ge(s_do.h, attn_done_do)
            s_ldg.inc(gp.dma_start(out=SW2B[:].rearrange("p (hc o) -> p hc o", hc=8), in_=sw2_h.rearrange("hc p o -> p hc o")), 16)
            s_ldg.inc(gp.dma_start(out=WUPB[:].rearrange("p (lc o) -> p lc o", lc=4), in_=wup_h.rearrange("lc p o -> p lc o")), 16)
            moew_done = s_ldg.n
            te.wait_ge(s_ldc.h, reload_done)
            for tt in range(8):
                bank = pb[6 + tt % 2]
                if tt >= 2:
                    te.wait_ge(s_dlm.h, tt - 1)
                for lc in range(4):
                    mm = te.transpose(bank[:, lc * 128:(lc + 1) * 128],
                                      ACCO[:, tt * 512 + lc * 128:tt * 512 + (lc + 1) * 128],
                                      identf_s[:])
                s_pe.inc(mm)
                ve.wait_ge(s_pe.h, s_pe.n)
                dst = ACCLM[:, :].rearrange("p (lc t) -> p lc t", lc=4)[:, :, tt * 128:(tt + 1) * 128]
                s_dlm.inc(ve.tensor_copy(dst, bank[:, 0:512].rearrange("p (a b) -> p a b", a=4)))
            acclm_done = s_dlm.n
            te.wait_ge(s_ldg.h, moew_done)
            te.wait_ge(s_dlm.h, acclm_done)
            gi = 0
            for tt in range(8):
                for ob in range(4):
                    bank = pb[gi % 2]
                    if gi >= 2:
                        te.wait_ge(s_ag.h, gi - 1)
                    for hc in range(8):
                        mm = te.matmul(bank[:], USO[:, hc * 1024 + tt * 128:hc * 1024 + (tt + 1) * 128],
                                       SW2B[:, hc * 2048 + ob * 512:hc * 2048 + (ob + 1) * 512],
                                       start=(hc == 0), stop=False)
                    for lc in range(4):
                        mm = te.matmul(bank[:], ACCLM[:, lc * 1024 + tt * 128:lc * 1024 + (tt + 1) * 128],
                                       WUPB[:, lc * 2048 + ob * 512:lc * 2048 + (ob + 1) * 512],
                                       start=False, stop=(lc == 3))
                    s_pmm.inc(mm)
                    se.wait_ge(s_pmm.h, s_pmm.n)
                    if gi == 0:
                        se.wait_ge(s_do.h, attn_done_do)
                    s_ag.inc(se.activation(out=GS[:, tt * 2048 + ob * 512:tt * 2048 + (ob + 1) * 512],
                                           in_=bank[:], func=AF.Silu))
                    gi += 1
            g_done = s_ag.n

            nc.leave_named_scope("ph_moeout", _scph_moeout[0], False)
            _scph_scomb = nc.enter_named_scope("ph_scomb", False)
            sy.wait_ge(s_cc.h, 1)
            pid = sy.partition_id()
            atk_ld = {}

            def atok_load(r):
                bslot = r % 2
                src_ap = agout_d.ap()[bass.ds(pid * 1024 + r * TOK, 1024), :]
                s_ldc.inc(sy.dma_start(
                    out=ATOK[:, bslot * 2048:(bslot + 1) * 2048]
                    .rearrange("p (tl ch) -> p tl ch", tl=8),
                    in_=src_ap.rearrange("(tl p) ch -> p tl ch", p=128)), 16)
                atk_ld[r] = s_ldc.n

            atok_load(0)
            atok_load(1)
            cmg = 0
            for r in range(8):
                bslot = r % 2
                te.wait_ge(s_ldc.h, atk_ld[r])
                if r + 2 < 8:
                    sy.wait_ge(s_pe.h, 0)
                for blk in range(2):
                    for grp in range(2):
                        bank = pb[2 + cmg % 2][:].bitcast(BF16)
                        if cmg >= 2:
                            te.wait_ge(s_dcm.h, cmg - 1)
                        for q in range(4):
                            tl = grp * 4 + q
                            mm = te.transpose(bank[:, q * 128:(q + 1) * 128],
                                              ATOK[:, bslot * 2048 + tl * 256 + blk * 128:bslot * 2048 + tl * 256 + (blk + 1) * 128],
                                              identb_s[:])
                        s_pe.inc(mm)
                        ve.wait_ge(s_pe.h, s_pe.n)
                        if cmg == 0:
                            ve.wait_ge(s_ag.h, g_done)
                        s_dcm.inc(ve.tensor_copy(
                            ACM[:, (2 * r + blk) * 1024 + grp * 512:(2 * r + blk) * 1024 + (grp + 1) * 512],
                            bank[:, 0:512]))
                        cmg += 1
                acm_tp_done[r] = s_pe.n
                if r + 2 < 8:
                    sy.wait_ge(s_pe.h, acm_tp_done[r])
                    atok_load(r + 2)
            acm_done = s_dcm.n
            wp_ld = {}

            def wp_load(ob):
                s_ldg.inc(gp.dma_start(out=WPB[:, (ob % 2) * 8192:(ob % 2 + 1) * 8192]
                                       .rearrange("p (cc o) -> p cc o", cc=16), in_=wproj_h[ob].rearrange("cc p o -> p cc o")), 16)
                wp_ld[ob] = s_ldg.n

            wp_load(0)
            wp_load(1)
            si = 0
            for ob in range(4):
                te.wait_ge(s_ldg.h, wp_ld[ob])
                if ob == 0:
                    te.wait_ge(s_dcm.h, acm_done)
                for tt in range(8):
                    bank = pb[4 + si % 2]
                    if si >= 2:
                        te.wait_ge(s_dco.h, si - 1)
                    xs = si % 2
                    if si >= 2:
                        sy.wait_ge(s_dco.h, si - 1)
                    s_ldx.inc(sy.dma_start(out=XSL[:, xs * 512:(xs + 1) * 512],
                                           in_=xown_h[tt * 128:(tt + 1) * 128, ob * 512:(ob + 1) * 512]), 16)
                    for cc in range(16):
                        mm = te.matmul(bank[:], ACM[:, cc * 1024 + tt * 128:cc * 1024 + (tt + 1) * 128],
                                       WPB[:, (ob % 2) * 8192 + cc * 512:(ob % 2) * 8192 + (cc + 1) * 512],
                                       start=(cc == 0), stop=(cc == 15))
                    s_pmm.inc(mm)
                    sxo_mm_done[ob] = s_pmm.n
                    ve.wait_ge(s_pmm.h, s_pmm.n)
                    ve.wait_ge(s_ldx.h, s_ldx.n)
                    if si == 0:
                        ve.wait_ge(s_ag.h, g_done)
                    if si >= 2:
                        ve.wait_ge(s_st.h, out_st_done[si - 2])
                    s_dve.inc(ve.tensor_tensor(TSL[:, xs * 512:(xs + 1) * 512], bank[:],
                                               GS[:, tt * 2048 + ob * 512:tt * 2048 + (ob + 1) * 512], ALU.mult))
                    ve.drain()
                    s_dco.inc(ve.tensor_tensor(OSL[:, xs * 512:(xs + 1) * 512],
                                               TSL[:, xs * 512:(xs + 1) * 512],
                                               XSL[:, xs * 512:(xs + 1) * 512], ALU.add))
                    sy.wait_ge(s_dco.h, s_dco.n)
                    s_st.inc(sy.dma_start(out=out_h[tt * 128:(tt + 1) * 128, ob * 512:(ob + 1) * 512],
                                          in_=OSL[:, xs * 512:(xs + 1) * 512]), 16)
                    out_st_done[si] = s_st.n
                    si += 1
                if ob + 2 < 4:
                    gp.wait_ge(s_pmm.h, sxo_mm_done[ob])
                    wp_load(ob + 2)
            sy.wait_ge(s_st.h, s_st.n)
            nc.leave_named_scope("ph_scomb", _scph_scomb[0], False)

    return nc


def _stage_inputs(L0_attn_x, L0_attn_W_attn, L0_attn_W_proj, L0_attn_w_std, L0_attn_w_rec,
                  L0_ffn_sw1, L0_ffn_swg, L0_ffn_sw2, L0_ffn_rw1, L0_ffn_rwg, L0_ffn_rw2,
                  L0_ffn_Wg, L0_ffn_Wdown, L0_ffn_Wup, L0_ffn_biases):
    f32 = np.float32
    asc = np.ascontiguousarray
    x = asc(np.asarray(L0_attn_x, f32).reshape(TOK, C))
    WA = np.asarray(L0_attn_W_attn, f32)

    wproj_st = asc(np.asarray(L0_attn_W_proj, f32).T.reshape(16, 128, 4, 512)
                   .transpose(2, 0, 1, 3))
    sw1_st = asc(np.asarray(L0_ffn_sw1, f32).T.reshape(16, 128, 8, 128).transpose(2, 0, 1, 3))
    swg_st = asc(np.asarray(L0_ffn_swg, f32).T.reshape(16, 128, 8, 128).transpose(2, 0, 1, 3))
    sw2_st = asc(np.asarray(L0_ffn_sw2, f32).T.reshape(8, 128, 2048))
    wdown_st = asc(np.asarray(L0_ffn_Wdown, f32).T.reshape(16, 128, 512))
    wup_st = asc(np.asarray(L0_ffn_Wup, f32).T.reshape(4, 128, 2048))
    wg_st = asc(np.asarray(L0_ffn_Wg, f32).T.reshape(16, 128, 16))
    rw1_st = asc(np.asarray(L0_ffn_rw1, f32).transpose(0, 2, 1).reshape(E, 4, 128, 1024))
    rwg_st = asc(np.asarray(L0_ffn_rwg, f32).transpose(0, 2, 1).reshape(E, 4, 128, 1024))
    rw2_st = asc(np.asarray(L0_ffn_rw2, f32).transpose(0, 2, 1).reshape(E, 8, 128, 512))
    biases = asc(np.asarray(L0_ffn_biases, f32).reshape(1, 16))
    w_std = np.asarray(L0_attn_w_std, f32)
    w_rec = np.asarray(L0_attn_w_rec, f32)

    in_maps = []
    for c in range(NCORE):
        h0 = 2 * c
        cols = []
        for h in (h0, h0 + 1):
            cols.append(WA[DSTD * h:DSTD * (h + 1)])
            cols.append(WA[5888 + LOWR * h:5888 + LOWR * (h + 1)])
        for h in (h0, h0 + 1):
            cols.append(WA[1920 + DSTD * h:1920 + DSTD * (h + 1)])
            cols.append(WA[6016 + LOWR * h:6016 + LOWR * (h + 1)])
        for h in (h0, h0 + 1):
            cols.append(WA[3840 + DH * h:3840 + DH * (h + 1)])
        wt_st = asc(np.concatenate(cols, axis=0).T.reshape(16, 128, 768))
        wlow = asc(np.concatenate([WA[5888 + LOWR * h0:5888 + LOWR * (h0 + 2)],
                                   WA[6016 + LOWR * h0:6016 + LOWR * (h0 + 2)]], axis=0))
        wsr = asc(np.stack([w_std[h0], w_std[h0 + 1], w_rec[h0], w_rec[h0 + 1]]).reshape(1, 4))
        xown = asc(x[c * NT:(c + 1) * NT])
        xfm = asc(xown.T.reshape(16, 128, 8, 128).transpose(2, 0, 1, 3))
        in_maps.append({
            "x": x, "xown": xown, "xfm": xfm, "wt_attn": wt_st, "wlow": wlow, "wsr": wsr,
            "wproj": wproj_st, "sw1": sw1_st, "swg": swg_st, "sw2": sw2_st,
            "wdown": wdown_st, "wup": wup_st, "wg": wg_st,
            "rw1": rw1_st, "rwg": rwg_st, "rw2": rw2_st, "biases": biases,
        })
    return in_maps


def kernel(**inputs):
    in_maps = _stage_inputs(**inputs)
    nc = build_program()
    res = run_bass_kernel_spmd(nc, in_maps, core_ids=list(range(NCORE)))
    out = np.concatenate([res.results[c]["out"] for c in range(NCORE)], axis=0)
    return out.reshape(B, T, C).astype(np.float32)
